# version 03
# baseline (speedup 1.0000x reference)
import numpy as np
from contextlib import ExitStack
import concourse.bass as bass
import concourse.mybir as mybir
from concourse.bass_utils import run_bass_kernel_spmd

F32 = mybir.dt.float32
AF = mybir.ActivationFunctionType
ALU = mybir.AluOpType
AX = mybir.AxisListType

D = 2048
T = 2048
CH = 16
NQT = 16
SPAN = 512
NSPAN = 4
H = 16
E = 8
R = 64
ALPHA = float(1.0 / np.sqrt(128.0))
EPS = 1e-5
MASKNEG = -1.0e30
INV_D = float(1.0 / D)
INV_RANK = float(1.0 / R)

ENGINES = ("sync", "tensor", "vector", "scalar", "gpsimd")


class _Sem:
    def __init__(self, name, handle):
        self.name = name
        self.h = handle
        self.n = 0


class _Ctx:
    def __init__(self, nc, sems):
        self.nc = nc
        self.streams = {e: [] for e in ENGINES}
        self.sems = sems
        self._lastw = {}
        self.strict = False

    def wait(self, eng, sem, val):
        if val is None or val <= 0:
            return
        key = (eng, sem.name)
        if self._lastw.get(key, 0) >= val:
            return
        self._lastw[key] = val
        self.streams[eng].append(lambda Eng, h=sem.h, v=val: Eng.wait_ge(h, v))

    def ins(self, eng, fn, sem=None, k=1, waits=()):
        if eng == "vector":
            self._vn = getattr(self, "_vn", 0) + 1
            import sys as _sys
            f = _sys._getframe(1)
            if not hasattr(self, "vlog"):
                self.vlog = {}
            self.vlog[self._vn] = f.f_lineno
        do_strict = (self.strict is True or
                     (isinstance(self.strict, (set, frozenset)) and eng in self.strict) or
                     (isinstance(self.strict, tuple) and eng == "vector"
                      and self.strict[0] <= self._vn < self.strict[1]))
        if do_strict:
            for s in self.sems.values():
                self.wait(eng, s, s.n)
        for (s, v) in waits:
            self.wait(eng, s, v)
        if sem is None:
            self.streams[eng].append(fn)
            return None
        sem.n += k
        v = sem.n
        self.streams[eng].append(lambda Eng, f=fn, h=sem.h, kk=k: f(Eng).then_inc(h, kk))
        return v

    def dma(self, eng, out_ap, in_ap, sem, waits=(), slow=False):
        if slow:
            return self.ins(eng, lambda Eng, o=out_ap, i=in_ap: Eng.dma_start(
                out=o, in_=i, allow_slow_non_contiguous=True), sem=sem, k=16, waits=waits)
        return self.ins(eng, lambda Eng, o=out_ap, i=in_ap: Eng.dma_start(out=o, in_=i),
                        sem=sem, k=16, waits=waits)

    def barrier(self):
        snap = {name: s.n for name, s in self.sems.items()}
        for eng in ENGINES:
            for name, v in snap.items():
                self.wait(eng, self.sems[name], v)


class _StopBuild(Exception):
    pass


def build_nc(debug_outputs=(), phases=4, strict=False):
    nc = bass.Bass("TRN2", target_bir_lowering=False)

    xT = nc.dram_tensor("xT", [D, T], F32, kind="ExternalInput").ap()
    wqT = nc.dram_tensor("wqT", [D, D], F32, kind="ExternalInput").ap()
    wkT = nc.dram_tensor("wkT", [D, D], F32, kind="ExternalInput").ap()
    wvT = nc.dram_tensor("wvT", [D, D], F32, kind="ExternalInput").ap()
    woT = nc.dram_tensor("woT", [D, D], F32, kind="ExternalInput").ap()
    aT = nc.dram_tensor("aT", [D, E * R], F32, kind="ExternalInput").ap()
    bS = nc.dram_tensor("bS", [E * R, D], F32, kind="ExternalInput").ap()
    wrT = nc.dram_tensor("wrT", [D, E], F32, kind="ExternalInput").ap()
    br = nc.dram_tensor("br", [1, E], F32, kind="ExternalInput").ap()
    outT = nc.dram_tensor("outT", [D, T], F32, kind="ExternalOutput").ap()

    def scratch(name):
        kind = "ExternalOutput" if name in debug_outputs else "Internal"
        return nc.dram_tensor(name + "_s", [D, T], F32, kind=kind).ap()

    qTs = scratch("q")
    kTs = scratch("k")
    vS = scratch("v")
    x1Ts = scratch("x1")
    attnS = scratch("attn")
    dbgP = nc.dram_tensor("dbgP_s", [3 * 128, T], F32,
                          kind="ExternalOutput" if ("p2dbg" in debug_outputs or "p4dbg" in debug_outputs) else "Internal").ap()
    dbgC = nc.dram_tensor("dbgC_s", [128, 16], F32,
                          kind="ExternalOutput" if ("p2dbg" in debug_outputs or "p4dbg" in debug_outputs) else "Internal").ap()

    es = ExitStack()
    sems = {}
    for name in ("LD", "ST", "PE", "DV", "SC", "GP"):
        sems[name] = _Sem(name, es.enter_context(nc.semaphore(name)))
    LD, ST, PE, DV, SC, GP = (sems[k] for k in ("LD", "ST", "PE", "DV", "SC", "GP"))
    ctx = _Ctx(nc, sems)
    ctx.strict = strict

    def sb(name, shape, stack=es):
        return stack.enter_context(nc.sbuf_tensor(name, shape, F32))

    def psum(name, shape, stack=es):
        return stack.enter_context(nc.psum_tensor(name, shape, F32))

    ident = sb("ident", [128, 128])
    cmask = sb("cmask", [128, 128])
    sel8 = sb("sel8", [8, 512])
    selE = sb("selE", [8, 512])
    onesD = sb("onesD", [128, 128])
    ones1 = sb("ones1", [128, 128])
    onec = sb("onec", [1, 128])
    onescol = sb("onescol", [128, 1])
    one1 = sb("one1", [1, 1])
    br_bc = sb("br_bc", [128, E])
    br_row = sb("br_row", [1, E])
    ioA = sb("ioA", [128, 128])
    ioB = sb("ioB", [128, 128])
    epscol = sb("epscol", [128, 1])

    ctx.ins("gpsimd", lambda Eg: Eg.iota(ioA[:, :], [[1, 128]], channel_multiplier=0,
                                         allow_small_or_imprecise_dtypes=True))
    ctx.ins("gpsimd", lambda Eg: Eg.iota(ioB[:, :], [[0, 128]], channel_multiplier=1,
                                         allow_small_or_imprecise_dtypes=True))
    ctx.ins("gpsimd", lambda Eg: Eg.iota(sel8[:, :], [[1, 8], [0, 64]], channel_multiplier=0,
                                         allow_small_or_imprecise_dtypes=True))
    ctx.ins("gpsimd", lambda Eg: Eg.iota(selE[:, :], [[0, 8], [0, 64]], channel_multiplier=1,
                                         allow_small_or_imprecise_dtypes=True), sem=GP)
    g_iota = GP.n
    ctx.ins("vector", lambda Ev: Ev.tensor_tensor(ident[:, :], ioA[:, :], ioB[:, :], ALU.is_equal),
            waits=((GP, g_iota),))
    ctx.ins("vector", lambda Ev: Ev.tensor_tensor(cmask[:, :], ioA[:, :], ioB[:, :], ALU.is_gt))
    ctx.ins("vector", lambda Ev: Ev.tensor_scalar_mul(cmask[:, :], cmask[:, :], MASKNEG))
    ctx.ins("vector", lambda Ev: Ev.tensor_tensor(sel8[:, :], sel8[:, :], selE[:, :], ALU.is_equal))
    ctx.ins("vector", lambda Ev: Ev.memset(onesD[:, :], INV_D))
    ctx.ins("vector", lambda Ev: Ev.memset(ones1[:, :], 1.0))
    ctx.ins("vector", lambda Ev: Ev.memset(onec[:, :], 1.0))
    ctx.ins("vector", lambda Ev: Ev.memset(epscol[:, :], EPS))
    ctx.ins("vector", lambda Ev: Ev.memset(onescol[:, :], 1.0))
    ctx.ins("vector", lambda Ev: Ev.memset(one1[:, :], 1.0), sem=DV)
    g_consts = DV.n

    attn_stack = ExitStack()
    try:
      with ExitStack() as p1o:
          xsb = sb("xsb", [128, CH, T], p1o)
          r1bc = sb("r1bc", [128, T], p1o)
          r1col = sb("r1col", [128, NQT], p1o)

          with ExitStack() as p1a:
              xsq = sb("xsq", [128, 2, T], p1a)
              m1bc = sb("m1bc", [128, T], p1a)
              st_ps = psum("st_ps", [128, 8, SPAN], p1a)

              x_ld = []
              for c in range(CH):
                  x_ld.append(ctx.dma("sync", xsb[:, c, :], xT[c * 128:(c + 1) * 128, :], LD))
              sq_done = [0, 0]
              last_s2 = 0
              for c in range(CH):
                  for sp in range(NSPAN):
                      ctx.ins("tensor", lambda Et, c=c, sp=sp: Et.matmul(
                          st_ps[:, sp, :], onesD[:, :], xsb[:, c, sp * SPAN:(sp + 1) * SPAN],
                          start=(c == 0), stop=(c == CH - 1)),
                          waits=((LD, x_ld[c]), (DV, g_consts)))
                  par = c % 2
                  sqv = ctx.ins("scalar", lambda Es, c=c, par=par: Es.activation(
                      xsq[:, par, :], xsb[:, c, :], AF.Square),
                      sem=SC, waits=((LD, x_ld[c]), (PE, sq_done[par])))
                  for sp in range(NSPAN):
                      v = ctx.ins("tensor", lambda Et, par=par, sp=sp, c=c: Et.matmul(
                          st_ps[:, 4 + sp, :], onesD[:, :], xsq[:, par, sp * SPAN:(sp + 1) * SPAN],
                          start=(c == 0), stop=(c == CH - 1)),
                          sem=(PE if sp == NSPAN - 1 else None), waits=((SC, sqv),))
                      if sp == NSPAN - 1:
                          sq_done[par] = v
                          last_s2 = v

              for sp in range(NSPAN):
                  ctx.ins("vector", lambda Ev, sp=sp: Ev.tensor_copy(
                      m1bc[:, sp * SPAN:(sp + 1) * SPAN], st_ps[:, sp, :]),
                      waits=((PE, last_s2),))
                  ctx.ins("vector", lambda Ev, sp=sp: Ev.tensor_mul(
                      xsq[:, 0, sp * SPAN:(sp + 1) * SPAN],
                      m1bc[:, sp * SPAN:(sp + 1) * SPAN], m1bc[:, sp * SPAN:(sp + 1) * SPAN]))
                  ctx.ins("vector", lambda Ev, sp=sp: Ev.tensor_tensor(
                      r1bc[:, sp * SPAN:(sp + 1) * SPAN], st_ps[:, 4 + sp, :],
                      xsq[:, 0, sp * SPAN:(sp + 1) * SPAN], ALU.subtract), sem=DV)
              ctx.ins("vector", lambda Ev: Ev.tensor_scalar_add(r1bc[:, :], r1bc[:, :], EPS),
                      sem=DV)
              g_var = DV.n
              sqv = ctx.ins("scalar", lambda Es: Es.activation(
                  r1bc[:, :], r1bc[:, :], AF.Sqrt), sem=SC, waits=((DV, g_var),))
              ctx.ins("vector", lambda Ev: Ev.reciprocal(r1bc[:, :], r1bc[:, :]),
                      sem=DV, waits=((SC, sqv),))
              g_r1 = DV.n
              cp = g_r1
              for i in range(NQT):
                  v = ctx.ins("tensor", lambda Et, i=i: Et.matmul(
                      st_ps[:, 0, 0:1], r1bc[0:1, i * 128:(i + 1) * 128], one1[:, :],
                      start=True, stop=True), sem=PE, waits=((DV, cp),))
                  cp = ctx.ins("vector", lambda Ev, i=i: Ev.tensor_copy(
                      r1col[:, i:i + 1], st_ps[:, 0, 0:1]), sem=DV, waits=((PE, v),))
              for c in range(CH):
                  ctx.ins("vector", lambda Ev, c=c: Ev.tensor_tensor(
                      xsb[:, c, :], xsb[:, c, :], m1bc[:, :], ALU.subtract), sem=DV)
              ctx.barrier()

          with ExitStack() as p1b:
              wbuf = sb("wbuf", [128, 2, CH, 128], p1b)
              stage = sb("stage", [128, 2, T], p1b)
              pp = psum("pp", [128, 2, SPAN], p1b)
              wb_free = [0, 0]
              st_free = [0, 0]
              bank_free = [0, 0]
              for wsrc, wdst in ((wqT, qTs), (wkT, kTs)):
                  for oc in range(CH):
                      par = oc % 2
                      wv = ctx.dma("sync", wbuf[:, par, :, :],
                                   wsrc[:, oc * 128:(oc + 1) * 128].rearrange("(c p) o -> p c o", p=128),
                                   LD, waits=((PE, wb_free[par]),))
                      evs = 0
                      for sp in range(NSPAN):
                          bank = sp % 2
                          for c in range(CH):
                              v = ctx.ins("tensor", lambda Et, par=par, c=c, sp=sp, bank=bank: Et.matmul(
                                  pp[:, bank, :], wbuf[:, par, c, :],
                                  xsb[:, c, sp * SPAN:(sp + 1) * SPAN],
                                  start=(c == 0), stop=(c == CH - 1)),
                                  sem=(PE if c == CH - 1 else None),
                                  waits=((LD, wv), (DV, bank_free[bank])))
                          mmv = v
                          if sp == NSPAN - 1:
                              wb_free[par] = mmv
                          bank_free[bank] = ctx.ins("vector", lambda Ev, par=par, sp=sp, bank=bank: Ev.tensor_tensor(
                              stage[:, par, sp * SPAN:(sp + 1) * SPAN], pp[:, bank, :],
                              r1bc[:, sp * SPAN:(sp + 1) * SPAN], ALU.mult),
                              sem=DV, waits=((PE, mmv), (ST, st_free[par])))
                          evs = bank_free[bank]
                      st_free[par] = ctx.dma("gpsimd", wdst[oc * 128:(oc + 1) * 128, :],
                                             stage[:, par, :], ST, waits=((DV, evs),))
              ctx.barrier()

          with ExitStack() as p1c:
              vw = sb("vw", [128, CH, SPAN], p1c)
              vstage = sb("vstage", [128, 2, SPAN], p1c)
              ppv = psum("ppv", [128, 2, SPAN], p1c)
              vw_free = 0
              vst_free = [0, 0]
              vbank_free = [0, 0]
              for os_ in range(4):
                  wv = ctx.dma("sync", vw[:, :, :],
                               wvT[:, os_ * SPAN:(os_ + 1) * SPAN].rearrange("(c p) o -> p c o", p=128),
                               LD, waits=((PE, vw_free),))
                  for tt in range(NQT):
                      bank = tt % 2
                      for c in range(CH):
                          v = ctx.ins("tensor", lambda Et, c=c, tt=tt, bank=bank: Et.matmul(
                              ppv[:, bank, :], xsb[:, c, tt * 128:(tt + 1) * 128], vw[:, c, :],
                              start=(c == 0), stop=(c == CH - 1)),
                              sem=(PE if c == CH - 1 else None),
                              waits=((LD, wv), (DV, vbank_free[bank])))
                      mmv = v
                      if tt == NQT - 1:
                          vw_free = mmv
                      spar = tt % 2
                      vbank_free[bank] = ctx.ins("vector", lambda Ev, spar=spar, bank=bank, tt=tt: Ev.tensor_scalar(
                          vstage[:, spar, :], ppv[:, bank, :], r1col[:, tt:tt + 1], None, ALU.mult),
                          sem=DV, waits=((PE, mmv), (ST, vst_free[spar])))
                      vst_free[spar] = ctx.dma(
                          "gpsimd", vS[tt * 128:(tt + 1) * 128, os_ * SPAN:(os_ + 1) * SPAN],
                          vstage[:, spar, :], ST, waits=((DV, vbank_free[bank]),))
              p1_store_total = ST.n
              ctx.barrier()
      if phases <= 1:
          raise _StopBuild

      with ExitStack() as p2:
          attn = sb("attn", [128, CH, T], attn_stack)
          qh = sb("qh", [128, 2, T], p2)
          kh = sb("kh", [128, 2, T], p2)
          vh = sb("vh", [128, 2, NQT, 128], p2)
          Pb = sb("Pb", [128, 2, T], p2)
          PTb = sb("PTb", [128, 2, SPAN], p2)
          mcolS = [sb("mcol0", [128, 1], p2), sb("mcol1", [128, 1], p2)]
          lpartS = [sb("lpart0", [128, 4], p2), sb("lpart1", [128, 4], p2)]
          ncolS = [sb("ncol0", [128, 1], p2), sb("ncol1", [128, 1], p2)]
          ninvS = [sb("ninv0", [128, 1], p2), sb("ninv1", [128, 1], p2)]
          nlnS = [sb("nln0", [128, 1], p2), sb("nln1", [128, 1], p2)]
          mbiasS = [sb("mbias0", [128, 1], p2), sb("mbias1", [128, 1], p2)]
          ninvD = [sb("ninvD0", [128, 1], p2), sb("ninvD1", [128, 1], p2)]
          diagb = [sb("diag0", [128, 128], p2), sb("diag1", [128, 128], p2)]
          pmax = sb("pmax", [128, 4], p2)
          S_ps = psum("S_ps", [128, 4 * SPAN], p2)
          PT_ps = psum("PT_ps", [128, 2, SPAN], p2)
          PV_ps = psum("PV_ps", [128, 2, SPAN], p2)

          for j in range(2):
              ctx.ins("vector", lambda Ev, j=j: Ev.memset(mcolS[j][:, :], 0.0))
              ctx.ins("vector", lambda Ev, j=j: Ev.memset(lpartS[j][:, :], 0.0))
              ctx.ins("vector", lambda Ev, j=j: Ev.memset(ncolS[j][:, :], 0.0))
              ctx.ins("vector", lambda Ev, j=j: Ev.memset(ninvS[j][:, :], 0.0))

          sub_rel = [0, 0, 0, 0]
          p_slot_rel = [0, 0]
          pt_rel = [0, 0]
          ptsb_rel = [0, 0]
          pv_rel = [0, 0]
          hd_free = [0, 0]
          q_ld = [0, 0]
          k_ld = [0, 0]
          v_ld = [0, 0]
          niv_n = [0, 0]

          def emit_B(h, i, par, pr):
              islot = i % 2
              span_par = (i // 4) % 2
              nkt = i + 1
              last_tp = 0
              last_pv = 0
              for jb in range(0, nkt, 4):
                  jend = min(jb + 4, nkt)
                  half = (jb // 4) % 2
                  for j in range(jb, jend):
                      v = ctx.ins("tensor", lambda Et, islot=islot, j=j, jb=jb, half=half: Et.matmul(
                          PT_ps[:, half, (j - jb) * 128:(j - jb + 1) * 128],
                          Pb[:, islot, j * 128:(j + 1) * 128], diagb[islot][:, :],
                          start=True, stop=True),
                          sem=(PE if j == jend - 1 else None),
                          waits=((DV, pr), (SC, pt_rel[half])))
                  last_tp = v
                  pt_rel[half] = ctx.ins("scalar", lambda Es, half=half, n=(jend - jb): Es.copy(
                      PTb[:, half, 0:n * 128], PT_ps[:, half, 0:n * 128]),
                      sem=SC, waits=((PE, max(last_tp, ptsb_rel[half])),))
                  for j in range(jb, jend):
                      v = ctx.ins("tensor", lambda Et, j=j, jb=jb, half=half, i=i, span_par=span_par, par=par: Et.matmul(
                          PV_ps[:, span_par, (i % 4) * 128:(i % 4 + 1) * 128],
                          vh[:, par, j, :], PTb[:, half, (j - jb) * 128:(j - jb + 1) * 128],
                          start=(j == 0), stop=(j == i)),
                          sem=(PE if j == jend - 1 else None),
                          waits=((SC, max(pt_rel[half], pv_rel[span_par])), (LD, v_ld[par])))
                  ptsb_rel[half] = v
                  last_pv = v
              p_slot_rel[islot] = last_tp
              if i % 4 == 3:
                  sp = i // 4
                  pv_rel[span_par] = ctx.ins("scalar", lambda Es, h=h, sp=sp, span_par=span_par: Es.copy(
                      attn[:, h, sp * SPAN:(sp + 1) * SPAN], PV_ps[:, span_par, :]),
                      sem=SC, waits=((PE, last_pv),))

          for h in range(H):
              par = h % 2
              q_ld[par] = ctx.dma("sync", qh[:, par, :], qTs[h * 128:(h + 1) * 128, :], LD,
                                  waits=((PE, hd_free[par]), (ST, p1_store_total)))
              k_ld[par] = ctx.dma("sync", kh[:, par, :], kTs[h * 128:(h + 1) * 128, :], LD)
              v_ld[par] = ctx.dma("sync", vh[:, par, :, :],
                                  vS[:, h * 128:(h + 1) * 128].rearrange("(c p) o -> p c o", p=128),
                                  LD)
              prev_B = None
              prev = None
              prev_B_args = None
              for i in range(NQT):
                  islot = i % 2
                  ktot = (i + 1) * 128
                  nks = (ktot + SPAN - 1) // SPAN
                  vks = []
                  for ks in range(nks):
                      nk = min(SPAN, ktot - ks * SPAN)
                      vks.append(ctx.ins("tensor", lambda Et, par=par, i=i, ks=ks, nk=nk: Et.matmul(
                          S_ps[:, ks * SPAN:ks * SPAN + nk], qh[:, par, i * 128:(i + 1) * 128],
                          kh[:, par, ks * SPAN:ks * SPAN + nk], start=True, stop=True),
                          sem=PE,
                          waits=((LD, max(q_ld[par], k_ld[par])), (DV, sub_rel[ks]))))
                  for ks in range(nks):
                      nk = min(SPAN, ktot - ks * SPAN)
                      ctx.ins("vector", lambda Ev, ks=ks, nk=nk: Ev.reduce_max(
                          pmax[:, ks:ks + 1], S_ps[:, ks * SPAN:ks * SPAN + nk], axis=AX.X),
                          waits=((PE, vks[ks]),))
                  if prev is not None:
                      pslot = prev[0]
                      ctx.ins("vector", lambda Ev, pslot=pslot: Ev.tensor_copy(
                          ninvD[pslot][:, :], ninvS[pslot][:, :]), waits=((SC, niv_n[pslot]),))
                  else:
                      ctx.ins("vector", lambda Ev: Ev.tensor_copy(mcolS[0][:, :], mcolS[0][:, :]))
                  ctx.ins("vector", lambda Ev, islot=islot, nks=nks: Ev.reduce_max(
                      mcolS[islot][:, :], pmax[:, 0:nks], axis=AX.X))
                  if prev is not None:
                      pslot, pktot = prev
                      pr_prev = ctx.ins("vector", lambda Ev, pslot=pslot: Ev.tensor_scalar(
                          diagb[pslot][:, :], ident[:, :], ninvD[pslot][:, :],
                          None, ALU.mult), sem=DV)
                      prev_B_args[3] = pr_prev
                  else:
                      ctx.ins("vector", lambda Ev: Ev.tensor_copy(pmax[:, 0:1], pmax[:, 0:1]))
                  for ks in range(nks):
                      nk = min(SPAN, ktot - ks * SPAN)
                      sub_rel[ks] = ctx.ins("vector", lambda Ev, islot=islot, ks=ks, nk=nk: Ev.tensor_scalar(
                          Pb[:, islot, ks * SPAN:ks * SPAN + nk],
                          S_ps[:, ks * SPAN:ks * SPAN + nk], mcolS[islot][:, :], ALPHA,
                          ALU.subtract, ALU.mult),
                          sem=DV, waits=((PE, p_slot_rel[islot]) if ks == 0 else (PE, 0),))
                  mp = ctx.ins("vector", lambda Ev, islot=islot, i=i: Ev.tensor_tensor(
                      Pb[:, islot, i * 128:(i + 1) * 128], Pb[:, islot, i * 128:(i + 1) * 128],
                      cmask[:, :], ALU.add), sem=DV)
                  if prev_B is not None:
                      prev_B()
                      prev_B = None
                  ev = ctx.ins("scalar", lambda Es, islot=islot, ktot=ktot: Es.activation(
                      Pb[:, islot, 0:ktot], Pb[:, islot, 0:ktot], AF.Exp,
                      accum_out=lpartS[islot][:, 0:1]),
                      sem=SC, waits=((DV, mp),))
                  ctx.ins("scalar", lambda Es, islot=islot: Es.activation(
                      nlnS[islot][:, :], lpartS[islot][:, 0:1], AF.Ln))
                  niv_n[islot] = ctx.ins("scalar", lambda Es, islot=islot: Es.activation(
                      ninvS[islot][:, :], nlnS[islot][:, :], AF.Exp, scale=-1.0), sem=SC)
                  prev_B_args = [h, i, par, None]
                  prev_B = (lambda a=prev_B_args: emit_B(a[0], a[1], a[2], a[3]))
                  prev = (islot, ktot)
              pslot, pktot = prev
              ctx.ins("vector", lambda Ev, pslot=pslot: Ev.tensor_copy(
                  ninvD[pslot][:, :], ninvS[pslot][:, :]), waits=((SC, niv_n[pslot]),))
              ctx.ins("vector", lambda Ev: Ev.tensor_copy(mcolS[0][:, :], mcolS[0][:, :]))
              pr_prev = ctx.ins("vector", lambda Ev, pslot=pslot: Ev.tensor_scalar(
                  diagb[pslot][:, :], ident[:, :], ninvD[pslot][:, :],
                  None, ALU.mult), sem=DV)
              prev_B_args[3] = pr_prev
              prev_B()
              prev_B = None
              hd_free[par] = PE.n
          ctx.barrier()
      if "attn" in debug_outputs:
          for c in range(CH):
              ctx.dma("gpsimd", attnS[c * 128:(c + 1) * 128, :], attn[:, c, :], ST)
      if phases <= 2:
          raise _StopBuild

      with ExitStack() as p3:
          wob = sb("wob", [128, 2, CH, 128], p3)
          xres = sb("xres", [128, 2, T], p3)
          stage3 = sb("stage3", [128, 2, T], p3)
          pp3 = psum("pp3", [128, 2, SPAN], p3)
          wb_free = [0, 0]
          st_free = [0, 0]
          bank_free = [0, 0]
          for oc in range(CH):
              par = oc % 2
              wv = ctx.dma("sync", wob[:, par, :, :],
                           woT[:, oc * 128:(oc + 1) * 128].rearrange("(c p) o -> p c o", p=128),
                           LD, waits=((PE, wb_free[par]),))
              xv = ctx.dma("sync", xres[:, par, :], xT[oc * 128:(oc + 1) * 128, :], LD)
              evs = 0
              for sp in range(NSPAN):
                  bank = sp % 2
                  for c in range(CH):
                      v = ctx.ins("tensor", lambda Et, par=par, c=c, sp=sp, bank=bank: Et.matmul(
                          pp3[:, bank, :], wob[:, par, c, :],
                          attn[:, c, sp * SPAN:(sp + 1) * SPAN],
                          start=(c == 0), stop=(c == CH - 1)),
                          sem=(PE if c == CH - 1 else None),
                          waits=((LD, wv), (DV, bank_free[bank])))
                  mmv = v
                  if sp == NSPAN - 1:
                      wb_free[par] = mmv
                  bank_free[bank] = ctx.ins("vector", lambda Ev, par=par, sp=sp, bank=bank: Ev.tensor_tensor(
                      stage3[:, par, sp * SPAN:(sp + 1) * SPAN], pp3[:, bank, :],
                      xres[:, par, sp * SPAN:(sp + 1) * SPAN], ALU.add),
                      sem=DV, waits=((PE, mmv), (LD, xv), (ST, st_free[par])))
                  evs = bank_free[bank]
              st_free[par] = ctx.dma("gpsimd", x1Ts[oc * 128:(oc + 1) * 128, :],
                                     stage3[:, par, :], ST, waits=((DV, evs),))
          p3_store_total = ST.n
          ctx.barrier()
      attn_stack.close()
      if phases <= 3:
          raise _StopBuild

      with ExitStack() as p4o:
          aTb = sb("aTb", [128, CH, E * R], p4o)
          bSb = sb("bSb", [128, 4, D], p4o)
          wrb = sb("wrb", [128, CH, E], p4o)
          rows2 = sb("rows2", [1, 3, T], p4o)
          r2col = sb("r2col", [128, NQT], p4o)
          nwA = sb("nwA", [1, E * R], p4o)
          nwR = sb("nwR", [1, E], p4o)

          aT_ld = ctx.dma("sync", aTb[:, :, :], aT.rearrange("(c p) o -> p c o", p=128), LD)
          bS_ld = ctx.dma("sync", bSb[:, :, :], bS.rearrange("(c p) o -> p c o", p=128), LD)
          wr_ld = ctx.dma("sync", wrb[:, :, :], wrT.rearrange("(c p) o -> p c o", p=128), LD)
          br_ld = ctx.dma("sync", br_row[:, :], br, LD, waits=((DV, g_consts),))

          with ExitStack() as p4a:
              x1tmp = sb("x1tmp", [128, 2, T], p4a)
              xsq4 = sb("xsq4", [128, 2, T], p4a)
              st4 = psum("st4", [128, 8, SPAN], p4a)
              sq_done = [0, 0]
              last_s2 = 0
              x1_ld = []
              for c in range(CH):
                  par = c % 2
                  x1_ld.append(ctx.dma("sync", x1tmp[:, par, :], x1Ts[c * 128:(c + 1) * 128, :], LD,
                                       waits=((PE, sq_done[par]), (ST, p3_store_total))))
                  for sp in range(NSPAN):
                      ctx.ins("tensor", lambda Et, c=c, sp=sp, par=par: Et.matmul(
                          st4[:, sp, :], onesD[:, :], x1tmp[:, par, sp * SPAN:(sp + 1) * SPAN],
                          start=(c == 0), stop=(c == CH - 1)),
                          waits=((LD, x1_ld[c]),))
                  sqv = ctx.ins("scalar", lambda Es, c=c, par=par: Es.activation(
                      xsq4[:, par, :], x1tmp[:, par, :], AF.Square),
                      sem=SC, waits=((LD, x1_ld[c]),))
                  for sp in range(NSPAN):
                      v = ctx.ins("tensor", lambda Et, par=par, sp=sp, c=c: Et.matmul(
                          st4[:, 4 + sp, :], onesD[:, :], xsq4[:, par, sp * SPAN:(sp + 1) * SPAN],
                          start=(c == 0), stop=(c == CH - 1)),
                          sem=(PE if sp == NSPAN - 1 else None), waits=((SC, sqv),))
                      if sp == NSPAN - 1:
                          sq_done[par] = v
                          last_s2 = v
              for sp in range(NSPAN):
                  ctx.ins("vector", lambda Ev, sp=sp: Ev.tensor_copy(
                      rows2[:, 0, sp * SPAN:(sp + 1) * SPAN], st4[0:1, sp, :]),
                      waits=((PE, last_s2),))
                  ctx.ins("vector", lambda Ev, sp=sp: Ev.tensor_mul(
                      rows2[:, 2, sp * SPAN:(sp + 1) * SPAN],
                      rows2[:, 0, sp * SPAN:(sp + 1) * SPAN],
                      rows2[:, 0, sp * SPAN:(sp + 1) * SPAN]))
                  ctx.ins("vector", lambda Ev, sp=sp: Ev.tensor_tensor(
                      rows2[:, 2, sp * SPAN:(sp + 1) * SPAN], st4[0:1, 4 + sp, :],
                      rows2[:, 2, sp * SPAN:(sp + 1) * SPAN], ALU.subtract), sem=DV)
              ctx.ins("vector", lambda Ev: Ev.tensor_scalar_add(rows2[:, 2, :], rows2[:, 2, :], EPS),
                      sem=DV)
              g_m2 = DV.n
              sqv = ctx.ins("scalar", lambda Es: Es.activation(
                  rows2[:, 2, :], rows2[:, 2, :], AF.Sqrt), sem=SC, waits=((DV, g_m2),))
              ctx.ins("vector", lambda Ev: Ev.reciprocal(rows2[:, 1, :], rows2[:, 2, :]),
                      sem=DV, waits=((SC, sqv),))
              g_r2 = DV.n
              cp = g_r2
              for i in range(NQT):
                  v = ctx.ins("tensor", lambda Et, i=i: Et.matmul(
                      st4[:, 0, 0:1], rows2[0:1, 1, i * 128:(i + 1) * 128], one1[:, :],
                      start=True, stop=True), sem=PE, waits=((DV, cp),))
                  cp = ctx.ins("vector", lambda Ev, i=i: Ev.tensor_copy(
                      r2col[:, i:i + 1], st4[:, 0, 0:1]), sem=DV, waits=((PE, v),))
              v = ctx.ins("tensor", lambda Et: Et.matmul(
                  st4[:, 1, 0:E], onec[:, :], br_row[:, :], start=True, stop=True),
                  sem=PE, waits=((LD, br_ld), (DV, g_m2)))
              ctx.ins("vector", lambda Ev: Ev.tensor_copy(br_bc[:, :], st4[:, 1, 0:E]),
                      sem=DV, waits=((PE, v),))
              for c in range(CH):
                  v = ctx.ins("tensor", lambda Et, c=c: Et.matmul(
                      st4[:, 2, 0:E * R], ones1[:, :], aTb[:, c, :],
                      start=(c == 0), stop=(c == CH - 1)),
                      sem=(PE if c == CH - 1 else None), waits=((LD, aT_ld), (DV, g_m2)))
              ctx.ins("vector", lambda Ev: Ev.tensor_scalar_mul(
                  nwA[:, :], st4[0:1, 2, 0:E * R], -1.0), sem=DV, waits=((PE, v),))
              for c in range(CH):
                  v = ctx.ins("tensor", lambda Et, c=c: Et.matmul(
                      st4[:, 3, 0:E], ones1[:, :], wrb[:, c, :],
                      start=(c == 0), stop=(c == CH - 1)),
                      sem=(PE if c == CH - 1 else None), waits=((LD, wr_ld), (DV, g_m2)))
              ctx.ins("vector", lambda Ev: Ev.tensor_scalar_mul(
                  nwR[:, :], st4[0:1, 3, 0:E], -1.0), sem=DV, waits=((PE, v),))
              g_rows = DV.n
              ctx.barrier()

          with ExitStack() as p4b:
              x1c = sb("x1c", [128, 2, CH, SPAN], p4b)
              hb = sb("hb", [128, 2, 4, SPAN], p4b)
              lgt = sb("lgt", [128, 2, E], p4b)
              sc_tok = sb("sc_tok", [128, 2, E], p4b)
              scT = sb("scT", [8, 2, SPAN], p4b)
              scbc = sb("scbc", [128, 2, 4, SPAN], p4b)
              tmax = sb("tmax", [128, 1], p4b)
              tsum = sb("tsum", [128, 1], p4b)
              tw = sb("tw", [128, 1], p4b)
              twr = sb("twr", [128, 1], p4b)
              vdm = sb("vdm", [128, 1], p4b)
              vdm2 = sb("vdm2", [128, 1], p4b)
              dbgA = sb("dbgA", [128, 8], p4b)
              dbgB = sb("dbgB", [128, 8], p4b)
              lgsb = sb("lgsb", [128, 8], p4b)
              ost = sb("ost", [128, 2, SPAN], p4b)
              for tz in (tmax, tsum, tw, twr, vdm, vdm2):
                  ctx.ins("vector", lambda Ev, tz=tz: Ev.memset(tz[:, :], 0.0))
              ctx.ins("vector", lambda Ev: Ev.memset(lgt[:, :, :], 0.0))
              ctx.ins("vector", lambda Ev: Ev.memset(sc_tok[:, :, :], 0.0))
              lg_ps = psum("lg_ps", [128, 2, SPAN], p4b)
              tp_ps = psum("tp_ps", [128, SPAN], p4b)
              h_ps = psum("h_ps", [128, 2, SPAN], p4b)
              o_ps = psum("o_ps", [128, 2, SPAN], p4b)

              x1_pe_free = [0, 0]
              x1_dv_free = [0, 0]
              h_free = [0, 0]
              hb_rdy = [0, 0]
              ost_free = [0, 0]
              obank_free = [0, 0]
              hbank_free = [0, 0]
              lg_rel = [0, 0]
              scT_free = [0, 0]
              scv_rel = [0, 0]
              lgtp_rel = 0
              sctp_rel = 0
              for sp in range(NSPAN):
                  par = sp % 2
                  xv = ctx.dma("sync", x1c[:, par, :, :],
                               x1Ts[:, sp * SPAN:(sp + 1) * SPAN].rearrange("(c p) t -> p c t", p=128),
                               LD, waits=((PE, x1_pe_free[par]), (DV, x1_dv_free[par]),
                                          (ST, p3_store_total)))
                  for ti in range(4):
                      tt = sp * 4 + ti
                      tslot = ti % 2
                      for c in range(CH):
                          ctx.ins("tensor", lambda Et, c=c, par=par, ti=ti, tslot=tslot: Et.matmul(
                              lg_ps[:, tslot, 0:E], x1c[:, par, c, ti * 128:(ti + 1) * 128],
                              wrb[:, c, :], start=(c == 0), stop=False),
                              waits=((LD, max(xv, wr_ld)),
                                     (DV, max(lg_rel[tslot], g_rows))))
                      v = ctx.ins("tensor", lambda Et, tt=tt, tslot=tslot: Et.matmul(
                          lg_ps[:, tslot, 0:E], rows2[0:1, 0, tt * 128:(tt + 1) * 128],
                          nwR[:, :], start=False, stop=True), sem=PE)
                      lg_rel[tslot] = ctx.ins("vector", lambda Ev, tt=tt, tslot=tslot: Ev.scalar_tensor_tensor(
                          lgt[:, tslot, :], lg_ps[:, tslot, 0:E], r2col[:, tt:tt + 1],
                          br_bc[:, :], ALU.mult, ALU.add), sem=DV,
                          waits=((PE, v), (SC, scv_rel[tslot])))
                      if "p4dbg" in debug_outputs and sp == 0 and ti == 0:
                          ctx.ins("vector", lambda Ev, tslot=tslot: Ev.tensor_copy(
                              dbgA[:, :], lgt[:, tslot, :]))
                      ctx.ins("vector", lambda Ev: Ev.tensor_copy(vdm[:, :], vdm2[:, :]))
                      ctx.ins("vector", lambda Ev, tslot=tslot: Ev.reduce_max(
                          tmax[:, :], lgt[:, tslot, :], axis=AX.X))
                      ctx.ins("vector", lambda Ev: Ev.tensor_copy(vdm[:, :], vdm2[:, :]))
                      ctx.ins("vector", lambda Ev, tslot=tslot: Ev.tensor_scalar(
                          sc_tok[:, tslot, :], lgt[:, tslot, :], tmax[:, :], None, ALU.is_equal))
                      lsub = ctx.ins("vector", lambda Ev, tslot=tslot: Ev.tensor_scalar(
                          lgt[:, tslot, :], lgt[:, tslot, :], tmax[:, :], None, ALU.subtract),
                          sem=DV)
                      scv_rel[tslot] = ctx.ins("scalar", lambda Es, tslot=tslot: Es.activation(
                          lgt[:, tslot, :], lgt[:, tslot, :], AF.Exp,
                          accum_out=tsum[:, :]), sem=SC, waits=((DV, lsub),))
                      ctx.ins("vector", lambda Ev: Ev.reciprocal(tw[:, :], tsum[:, :]),
                              waits=((SC, scv_rel[tslot]),))
                      ctx.ins("vector", lambda Ev, tt=tt: Ev.tensor_scalar(
                          twr[:, :], tw[:, :], r2col[:, tt:tt + 1], INV_RANK,
                          ALU.mult, ALU.mult))
                      ctx.ins("vector", lambda Ev: Ev.tensor_copy(vdm[:, :], vdm2[:, :]))
                      dvv = ctx.ins("vector", lambda Ev, tslot=tslot: Ev.tensor_scalar(
                          sc_tok[:, tslot, :], sc_tok[:, tslot, :], twr[:, :], None, ALU.mult),
                          sem=DV)
                      if "p4dbg" in debug_outputs and sp == 0 and ti == 0:
                          ctx.dma("gpsimd", dbgC[:, 0:8], sc_tok[:, tslot, :], ST,
                                  waits=((DV, dvv),), slow=True)
                          ctx.dma("gpsimd", dbgC[:, 8:9], tmax[:, :], ST, slow=True)
                          ctx.dma("gpsimd", dbgC[:, 9:10], twr[:, :], ST, slow=True)
                          ctx.dma("gpsimd", dbgC[:, 10:11], tw[:, :], ST, slow=True)
                          ctx.dma("gpsimd", dbgC[:, 11:12], r2col[:, 0:1], ST, slow=True)
                          ctx.dma("gpsimd", dbgP[0:128, 1024:1032], dbgA[:, :], ST,
                                  waits=((DV, lsub),))
                          ctx.dma("gpsimd", dbgP[0:128, 1056:1064], dbgB[:, :], ST,
                                  waits=((DV, lsub),))
                          ctx.dma("gpsimd", dbgP[0:128, 1040:1048], sc_tok[:, tslot, :], ST,
                                  waits=((DV, dvv),))
                      v = ctx.ins("tensor", lambda Et, tslot=tslot, ti=ti: Et.transpose(
                          tp_ps[0:8, ti * 128:(ti + 1) * 128], sc_tok[:, tslot, :], ident[:, :]),
                          sem=PE, waits=((SC, sctp_rel), (DV, dvv)))
                      sctp_rel = ctx.ins("scalar", lambda Es, par=par, ti=ti: Es.copy(
                          scT[0:8, par, ti * 128:(ti + 1) * 128],
                          tp_ps[0:8, ti * 128:(ti + 1) * 128]),
                          sem=SC, waits=((PE, max(v, scT_free[par])),))
                  scT_rdy = SC.n
                  if "p4dbg" in debug_outputs and sp == 0:
                      ctx.dma("gpsimd", dbgP[0:8, 0:SPAN], scT[0:8, par, :], ST,
                              waits=((SC, scT_rdy),))
                  for ec in range(4):
                      bank = ec % 2
                      v = ctx.ins("tensor", lambda Et, ec=ec, par=par, bank=bank: Et.matmul(
                          h_ps[:, bank, :], sel8[:, ec * 128:(ec + 1) * 128], scT[:, par, :],
                          start=True, stop=True), sem=PE,
                          waits=((SC, max(scT_rdy, hbank_free[bank])), (DV, hb_rdy[bank])))
                      scT_free[par] = v
                      hbank_free[bank] = ctx.ins("scalar", lambda Es, ec=ec, par=par, bank=bank: Es.copy(
                          scbc[:, par, ec, :], h_ps[:, bank, :]), sem=SC, waits=((PE, v),))
                  if "p4dbg" in debug_outputs and sp == 0:
                      ctx.dma("gpsimd", dbgP[128:256, 0:SPAN], scbc[:, par, 0, :], ST,
                              waits=((SC, hbank_free[0]),))
                  for ec in range(4):
                      bank = ec % 2
                      for c in range(CH):
                          ctx.ins("tensor", lambda Et, c=c, ec=ec, par=par, bank=bank: Et.matmul(
                              h_ps[:, bank, :], aTb[:, c, ec * 128:(ec + 1) * 128],
                              x1c[:, par, c, :], start=(c == 0), stop=False),
                              waits=((LD, max(xv, aT_ld)), (SC, hbank_free[bank]),
                                     (DV, hb_rdy[bank])))
                      v = ctx.ins("tensor", lambda Et, ec=ec, sp=sp, bank=bank: Et.matmul(
                          h_ps[:, bank, :], nwA[:, ec * 128:(ec + 1) * 128],
                          rows2[0:1, 0, sp * SPAN:(sp + 1) * SPAN], start=False, stop=True),
                          sem=PE)
                      hb_rdy[bank] = ctx.ins("vector", lambda Ev, ec=ec, par=par, bank=bank: Ev.tensor_tensor(
                          hb[:, par, ec, :], h_ps[:, bank, :], scbc[:, par, ec, :], ALU.mult),
                          sem=DV, waits=((PE, max(v, h_free[par])),))
                  h_rdy = DV.n
                  if "p4dbg" in debug_outputs and sp == 0:
                      ctx.dma("gpsimd", dbgP[256:384, 0:SPAN], hb[:, par, 0, :], ST,
                              waits=((DV, h_rdy),))
                  for oc in range(CH):
                      bank = oc % 2
                      opar = oc % 2
                      for ec in range(4):
                          v = ctx.ins("tensor", lambda Et, ec=ec, oc=oc, par=par, bank=bank: Et.matmul(
                              o_ps[:, bank, :], bSb[:, ec, oc * 128:(oc + 1) * 128],
                              hb[:, par, ec, :], start=(ec == 0), stop=(ec == 3)),
                              sem=(PE if ec == 3 else None),
                              waits=((DV, max(h_rdy, obank_free[bank])), (LD, bS_ld)))
                      mmv = v
                      if oc == CH - 1:
                          h_free[par] = mmv
                      obank_free[bank] = ctx.ins("vector", lambda Ev, oc=oc, par=par, bank=bank, opar=opar: Ev.tensor_tensor(
                          ost[:, opar, :], o_ps[:, bank, :], x1c[:, par, oc, :], ALU.add),
                          sem=DV, waits=((PE, mmv), (ST, ost_free[opar])))
                      ost_free[opar] = ctx.dma(
                          "gpsimd", outT[oc * 128:(oc + 1) * 128, sp * SPAN:(sp + 1) * SPAN],
                          ost[:, opar, :], ST, waits=((DV, obank_free[bank]),))
                  x1_pe_free[par] = PE.n
                  x1_dv_free[par] = DV.n

              ctx.wait("gpsimd", ST, ST.n)
    except _StopBuild:
        attn_stack.close()
        ctx.wait("gpsimd", ST, ST.n)

    with nc.Block() as block:
        for eng in ENGINES:
            def body(Eng, eng=eng):
                for fn in ctx.streams[eng]:
                    fn(Eng)
            getattr(block, eng)(body)
    es.close()
    return nc


def _prep_inputs(L0_attn_x, L0_attn_Wq, L0_attn_Wk, L0_attn_Wv, L0_attn_Wo,
                 L0_ffn_A, L0_ffn_B, L0_ffn_Wr, L0_ffn_br):
    C = np.ascontiguousarray
    f = np.float32
    wqT = C(np.asarray(L0_attn_Wq, dtype=f).T)
    wkT = C(np.asarray(L0_attn_Wk, dtype=f).T)
    wvT = C(np.asarray(L0_attn_Wv, dtype=f).T)
    woT = C(np.asarray(L0_attn_Wo, dtype=f).T)
    aT = C(np.asarray(L0_ffn_A, dtype=f).reshape(E * R, D).T)
    bSm = C(np.asarray(L0_ffn_B, dtype=f).transpose(0, 2, 1).reshape(E * R, D))
    wrT = C(np.asarray(L0_ffn_Wr, dtype=f).T)
    brm = C(np.asarray(L0_ffn_br, dtype=f).reshape(1, E))
    x = np.asarray(L0_attn_x, dtype=f)
    in_maps = []
    for b in range(8):
        in_maps.append({
            "xT": C(x[b].T),
            "wqT": wqT, "wkT": wkT, "wvT": wvT, "woT": woT,
            "aT": aT, "bS": bSm, "wrT": wrT, "br": brm,
        })
    return in_maps


def kernel(L0_attn_x, L0_attn_Wq, L0_attn_Wk, L0_attn_Wv, L0_attn_Wo,
           L0_ffn_A, L0_ffn_B, L0_ffn_Wr, L0_ffn_br):
    in_maps = _prep_inputs(L0_attn_x, L0_attn_Wq, L0_attn_Wk, L0_attn_Wv, L0_attn_Wo,
                           L0_ffn_A, L0_ffn_B, L0_ffn_Wr, L0_ffn_br)
    nc = build_nc()
    res = run_bass_kernel_spmd(nc, in_maps, core_ids=list(range(8)))
    out = np.stack([np.asarray(r["outT"], dtype=np.float32).T for r in res.results], axis=0)
    return np.ascontiguousarray(out)
